# version 01
# baseline (speedup 1.0000x reference)
import numpy as np
import ml_dtypes
from contextlib import ExitStack

import concourse.bass as bass
import concourse.mybir as mybir
from concourse.bass_utils import run_bass_kernel_spmd

F32 = mybir.dt.float32
BF16 = mybir.dt.bfloat16
AF = mybir.ActivationFunctionType
ALU = mybir.AluOpType
AX = mybir.AxisListType
BF = ml_dtypes.bfloat16

D = 1024
NH = 16
HD = 64
T = 2048
NT = T // 128
NC = 8
NCORES = 8
EPS_OUT = 1e-5
EPS_IN = 1e-6

ENGINES = ("sync", "gpsimd", "tensor", "scalar", "vector")


class Plan:

    FENCED = ("vector", "scalar")

    def __init__(self, nc):
        self.nc = nc
        self.ops = {e: [] for e in ENGINES}
        self.sems = {}
        self.handles = {}
        self.trace = {e: [] for e in ENGINES}
        self._last_eng = None
        self.last_op_idx = {e: None for e in ENGINES}
        self.last_inc = {e: None for e in ENGINES}

    def mksem(self, ctx, *names):
        for n in names:
            self.handles[n] = ctx.enter_context(self.nc.semaphore(n))
            self.sems[n] = 0

    def _append_wait(self, eng, name, v):
        h = self.handles[name]
        self.ops[eng].append(lambda e, h=h, v=v: e.wait_ge(h, v))
        self.trace[eng].append(("wait", name, v))

    def on(self, eng, fn, label=""):
        if eng in self.FENCED and self.last_op_idx[eng] is not None:
            if self.last_inc[eng] is None:
                self.tag(eng, f"_sf_{eng}")
            self._append_wait(eng, *self.last_inc[eng])
        self.ops[eng].append(fn)
        self.trace[eng].append(["op", label, []])
        self._last_eng = eng
        self.last_op_idx[eng] = len(self.ops[eng]) - 1
        self.last_inc[eng] = None

    def inc(self, name, by=1):
        self.sems[name] += by
        eng = self._last_eng
        self.trace[eng][self.last_op_idx[eng]][2].append((name, by))
        self.last_inc[eng] = (name, self.sems[name])
        return self.handles[name], self.sems[name]

    def tag(self, eng, name, by=1):
        i = self.last_op_idx[eng]
        fn = self.ops[eng][i]
        h = self.handles[name]
        self.ops[eng][i] = lambda e, fn=fn, h=h, by=by: fn(e).then_inc(h, by)
        self._last_eng = eng
        self.sems[name] += by
        self.trace[eng][i][2].append((name, by))
        self.last_inc[eng] = (name, self.sems[name])

    def val(self, name):
        return self.sems[name]

    def wait(self, eng, name, val=None):
        v = self.sems[name] if val is None else val
        if v <= 0:
            return
        self._append_wait(eng, name, v)

    def simulate(self):
        pos = {e: 0 for e in ENGINES}
        cnt = {n: 0 for n in self.sems}
        progress = True
        while progress:
            progress = False
            for e in ENGINES:
                while pos[e] < len(self.trace[e]):
                    ev = self.trace[e][pos[e]]
                    if ev[0] == "wait":
                        if cnt[ev[1]] < ev[2]:
                            break
                    else:
                        for name, by in ev[2]:
                            cnt[name] += by
                    pos[e] += 1
                    progress = True
        stuck = {e: (pos[e], len(self.trace[e])) for e in ENGINES
                 if pos[e] < len(self.trace[e])}
        if stuck:
            msg = []
            for e, (i, n) in stuck.items():
                ev = self.trace[e][i]
                msg.append(f"{e}@{i}/{n}: waiting {ev[1]}>={ev[2]} (now {cnt[ev[1]]})")
            raise RuntimeError("DEADLOCK in plan: " + "; ".join(msg))

    def emit(self, ctx):
        block = ctx.enter_context(self.nc.Block())
        plan = self

        @block.sync
        def _(e):
            for f in plan.ops["sync"]:
                f(e)

        @block.gpsimd
        def _(e):
            for f in plan.ops["gpsimd"]:
                f(e)

        @block.tensor
        def _(e):
            for f in plan.ops["tensor"]:
                f(e)

        @block.scalar
        def _(e):
            for f in plan.ops["scalar"]:
                f(e)

        @block.vector
        def _(e):
            for f in plan.ops["vector"]:
                f(e)



def build_l1():
    nc = bass.Bass("TRN2", target_bir_lowering=False)
    x0_d = nc.dram_tensor("x0st", [NT, 128, NC, 128], F32, kind="ExternalInput").ap()
    xp_d = nc.dram_tensor("xpst", [NT, 128, NC, 128], F32, kind="ExternalInput").ap()
    wq_d = nc.dram_tensor("wqkvT", [NC, 128, 3072], F32, kind="ExternalInput").ap()
    wo_d = nc.dram_tensor("woutT", [NC, 128, D], F32, kind="ExternalInput").ap()
    ind_d = nc.dram_tensor("ind", [128, 1], F32, kind="ExternalInput").ap()
    ident_d = nc.dram_tensor("identbf", [128, 128], BF16, kind="ExternalInput").ap()
    ones_d = nc.dram_tensor("onesbf", [128, 128], BF16, kind="ExternalInput").ap()
    onesf_d = nc.dram_tensor("onesf", [1, 128], F32, kind="ExternalInput").ap()
    x1_d = nc.dram_tensor("x1st", [NT, 128, NC, 128], F32, kind="ExternalOutput").ap()

    with ExitStack() as ctx:
        sb = lambda n, s, d=BF16: ctx.enter_context(nc.sbuf_tensor(n, s, d))
        psa = lambda n, s, d=F32: ctx.enter_context(nc.psum_tensor(n, s, d))

        KROW = psa("KROW", [1, 1024])
        S0 = psa("S0", [128, 512])
        S1 = psa("S1", [128, 512])
        S2 = psa("S2", [128, 512])
        S3 = psa("S3", [128, 512])
        CTX = psa("CTX", [128, 512])
        TRN = psa("TRN", [128, 512], BF16)

        wq_sb = sb("wq_sb", [128, NC * 3072])
        wo_sb = sb("wo_sb", [128, NC * 1024])
        attnT = sb("attnT", [128, NC * 2048])
        xo_sb = [sb(f"xo{j}", [128, NC * 128]) for j in range(2)]
        xp_sb = [sb(f"xp{j}", [128, NC * 128]) for j in range(2)]
        qx = sb("qx", [128, 1024])
        kx = sb("kx", [128, 1024])
        vx = sb("vx", [128, 1024])
        qsc = sb("qsc", [64, 1024])
        qT = sb("qT", [128, 512])
        snap = sb("snap", [128, 512])
        dp = sb("dp", [64, 1024])
        ctx_sb = sb("ctx_sb", [128, 512], F32)
        krow_f = sb("krow_f", [1, 1024], F32)
        krow_b = sb("krow_b", [1, 1024])
        qs = sb("qs", [128, 16], F32)
        qr = sb("qr", [128, 16], F32)
        qxb = sb("qxb", [64, 1024])
        qrb = sb("qrb", [64, 16], F32)
        dstat = sb("dstat", [64, 16], F32)
        dstat2 = sb("dstat2", [64, 16], F32)
        xres = sb("xres", [128, NC * 512], F32)
        sT = sb("sT", [128, NC * 512], F32)
        sbf = sb("sbf", [128, NC * 512])
        sqb = sb("sqb", [128, NC * 512])
        rowm = sb("rowm", [1, 512], F32)
        rowv = sb("rowv", [1, 512], F32)
        rowr = sb("rowr", [1, 512], F32)
        rowmr = sb("rowmr", [1, 512], F32)
        ident = sb("ident_sb", [128, 128])
        onesb = sb("onesb", [128, 128])
        onesf = sb("onesf_sb", [1, 128], F32)
        ind = sb("ind_sb", [128, 1], F32)

        p = Plan(nc)
        p.mksem(ctx, "_sf_vector", "_sf_scalar", "ldw", "ldwg", "ldx0", "ldx1", "ldp0", "ldp1", "mmq", "mmk", "mmv", "evq", "evk",
                "evv", "snp", "kce", "dsc", "trn", "qte", "app", "ate", "ctu",
                "kru", "app2", "pfx", "ldr", "mmo", "res", "sbc", "stm", "var", "rsr",
                "mrw", "rep", "nrm", "st")
        H = p.handles

        p.on("gpsimd", lambda e: e.dma_start(
            out=wq_sb[:, :].rearrange("p (c n) -> p c n", c=NC),
            in_=wq_d.rearrange("c p n -> p c n")).then_inc(H["ldwg"], 16))
        p.inc("ldwg", 16)
        p.on("gpsimd", lambda e: e.dma_start(
            out=wo_sb[:, :].rearrange("p (c n) -> p c n", c=NC),
            in_=wo_d.rearrange("c p n -> p c n")).then_inc(H["ldwg"], 16))
        p.inc("ldwg", 16)
        for dst, src in ((ident, ident_d), (onesb, ones_d), (onesf, onesf_d), (ind, ind_d)):
            p.on("sync", lambda e, d=dst, s=src: e.dma_start(
                out=d[:, :], in_=s[:, :]).then_inc(H["ldw"], 16))
            p.inc("ldw", 16)
        for eng in ("gpsimd", "tensor", "scalar", "vector"):
            p.wait(eng, "ldw")
            p.wait(eng, "ldwg")

        def emit_qkv(xbuf, col0, pa, pb):
            for blk, ps in ((0, pa), (1, pb)):
                for c in range(NC):
                    p.on("tensor", lambda e, b=blk, c=c, ps=ps, xb=xbuf, c0=col0: e.matmul(
                        ps[:, :],
                        xb[:, c * 128:(c + 1) * 128],
                        wq_sb[:, c * 3072 + c0 + b * 512: c * 3072 + c0 + b * 512 + 512],
                        start=(c == 0), stop=(c == 7)))

        for i in range(NT):
            buf = xp_sb[i % 2]
            p.wait("gpsimd", "ctu", p.val("ctu") - 1)
            sem_p = "ldp0" if i % 2 == 0 else "ldp1"
            p.on("gpsimd", lambda e, b=buf, i=i, s2=sem_p: e.dma_start(
                out=b[:, :], in_=xp_d[i]).then_inc(H[s2], 16))
            p.inc(sem_p, 16)
            p.wait("tensor", sem_p)
            if i > 0:
                p.wait("tensor", "evk")
                p.wait("tensor", "evv")
            emit_qkv(buf, 1024, S0, S1)
            emit_qkv(buf, 2048, S2, S3)
            p.tag("tensor", "mmk")
            p.wait("scalar", "mmk")
            p.wait("scalar", "ctu", p.val("ctu"))
            p.on("scalar", lambda e: e.activation(kx[:, 0:512], S0[:, :], AF.Exp))
            p.on("scalar", lambda e: e.activation(
                kx[:, 512:1024], S1[:, :], AF.Exp).then_inc(H["evk"], 1))
            p.inc("evk")
            p.wait("vector", "mmk")
            p.wait("vector", "ctu", p.val("ctu"))
            p.on("vector", lambda e: e.tensor_copy(vx[:, 0:512], S2[:, :]))
            p.on("vector", lambda e: e.tensor_copy(
                vx[:, 512:1024], S3[:, :]).then_inc(H["evv"], 1))
            p.inc("evv")
            p.wait("tensor", "evk")
            p.wait("tensor", "evv")
            if i > 0:
                p.wait("tensor", "ctu")
            for rb in (0, 64):
                if rb == 64:
                    p.wait("tensor", "ctu")
                for h in range(NH):
                    p.on("tensor", lambda e, rb=rb, h=h: e.matmul(
                        CTX[(h % 2) * 64:(h % 2) * 64 + 64,
                            (h // 2) * 64:(h // 2) * 64 + 64],
                        kx[rb:rb + 64, h * 64:h * 64 + 64],
                        vx[rb:rb + 64, h * 64:h * 64 + 64],
                        start=True, stop=True))
                for blk in range(2):
                    p.on("tensor", lambda e, b=blk, rb=rb: e.matmul(
                        KROW[0:1, b * 512:b * 512 + 512],
                        onesb[rb:rb + 64, 0:1],
                        kx[rb:rb + 64, b * 512:b * 512 + 512],
                        start=True, stop=True))
                p.tag("tensor", "kru")
                p.wait("vector", "kru")
                if i == 0 and rb == 0:
                    p.on("vector", lambda e: e.tensor_copy(ctx_sb[:, :], CTX[:, :]))
                    p.on("vector", lambda e: e.tensor_copy(krow_f[0:1, :], KROW[0:1, :]))
                else:
                    p.on("vector", lambda e: e.tensor_tensor(
                        ctx_sb[:, :], ctx_sb[:, :], CTX[:, :], ALU.add))
                    p.on("vector", lambda e: e.tensor_tensor(
                        krow_f[0:1, :], krow_f[0:1, :], KROW[0:1, :], ALU.add))
                p.tag("vector", "ctu")

        p.on("vector", lambda e: e.tensor_scalar(
            ctx_sb[:, :], ctx_sb[:, :], ind[:, 0:1], None, ALU.mult))
        p.on("vector", lambda e: e.tensor_scalar(
            krow_f[0:1, :], krow_f[0:1, :], ind[0:1, 0:1], None,
            ALU.mult).then_inc(H["pfx"], 1))
        p.inc("pfx")
        p.wait("tensor", "pfx")

        for i in range(NT):
            buf = xo_sb[i % 2]
            p.wait("gpsimd", "ctu", p.val("ctu") - 1)
            sem_x = "ldx0" if i % 2 == 0 else "ldx1"
            p.on("gpsimd", lambda e, b=buf, i=i, s2=sem_x: e.dma_start(
                out=b[:, :], in_=x0_d[i]).then_inc(H[s2], 16))
            p.inc(sem_x, 16)
            p.wait("tensor", sem_x)
            if i > 0:
                p.wait("tensor", "ate")
                p.wait("tensor", "dsc")
            emit_qkv(buf, 0, S0, S1)
            p.tag("tensor", "mmq")
            emit_qkv(buf, 1024, S2, S3)
            p.tag("tensor", "mmk")
            p.wait("scalar", "mmq")
            p.wait("scalar", "dsc", p.val("dsc"))
            p.on("scalar", lambda e: e.activation(qx[:, 0:512], S0[:, :], AF.Exp))
            p.on("scalar", lambda e: e.activation(
                qx[:, 512:1024], S1[:, :], AF.Exp).then_inc(H["evq"], 1))
            p.inc("evq")
            p.wait("scalar", "mmk")
            p.wait("scalar", "ctu", p.val("ctu"))
            p.on("scalar", lambda e: e.activation(kx[:, 0:512], S2[:, :], AF.Exp))
            p.on("scalar", lambda e: e.activation(
                kx[:, 512:1024], S3[:, :], AF.Exp).then_inc(H["evk"], 1))
            p.inc("evk")
            p.wait("tensor", "evq")
            emit_qkv(buf, 2048, S0, S1)
            p.tag("tensor", "mmv")
            p.wait("vector", "mmv")
            p.wait("vector", "ctu", p.val("ctu"))
            p.on("vector", lambda e: e.tensor_copy(vx[:, 0:512], S0[:, :]))
            p.on("vector", lambda e: e.tensor_copy(
                vx[:, 512:1024], S1[:, :]).then_inc(H["evv"], 1))
            p.inc("evv")
            p.wait("vector", "evq")
            p.on("vector", lambda e: e.tensor_reduce(
                qs[:, :], qx[:, :].rearrange("p (h d) -> p h d", d=HD), AX.X, ALU.add))
            p.on("vector", lambda e: e.reciprocal(qr[:, :], qs[:, :]))

            for b in (0, 1):
                rb = b * 64
                gcol = i * 128 + rb
                p.on("vector", lambda e: e.tensor_copy(snap[:, :], ctx_sb[:, :]))
                p.on("vector", lambda e: e.tensor_copy(
                    krow_b[0:1, :], krow_f[0:1, :]).then_inc(H["snp"], 1))
                p.inc("snp")
                p.wait("tensor", "snp")
                p.wait("tensor", "evk")
                for blk, ps in ((0, S2), (1, S3)):
                    p.on("tensor", lambda e, b2=blk, ps=ps: e.matmul(
                        ps[0:64, :], onesb[0:1, 0:64],
                        krow_b[0:1, b2 * 512:b2 * 512 + 512], start=True, stop=True))
                p.tag("tensor", "kce")
                p.wait("tensor", "ctu", p.val("ctu"))
                for blk in range(2):
                    p.on("tensor", lambda e, b2=blk, rb=rb: e.matmul(
                        KROW[0:1, b2 * 512:b2 * 512 + 512],
                        onesb[rb:rb + 64, 0:1],
                        kx[rb:rb + 64, b2 * 512:b2 * 512 + 512],
                        start=True, stop=True))
                p.tag("tensor", "kru")
                p.wait("vector", "kce")
                p.on("vector", lambda e, rb=rb: e.tensor_copy(qxb[:, :], qx[rb:rb + 64, :]))
                p.on("vector", lambda e, rb=rb: e.tensor_copy(qrb[:, :], qr[rb:rb + 64, :]))
                p.on("vector", lambda e: e.tensor_tensor(
                    dp[:, 0:512], qxb[:, 0:512], S2[0:64, :], ALU.mult))
                p.on("vector", lambda e: e.tensor_tensor(
                    dp[:, 512:1024], qxb[:, 512:1024], S3[0:64, :], ALU.mult))
                p.on("vector", lambda e: e.tensor_reduce(
                    dstat[:, :], dp[:, :].rearrange("p (h d) -> p h d", d=HD),
                    AX.X, ALU.add))
                p.on("vector", lambda e: e.tensor_tensor(
                    dstat[:, :], dstat[:, :], qrb[:, :], ALU.mult))
                p.on("vector", lambda e: e.tensor_scalar(
                    dstat[:, :], dstat[:, :], 0.125, 1e-6, ALU.mult, ALU.max))
                p.on("vector", lambda e: e.reciprocal(dstat2[:, :], dstat[:, :]))
                p.on("vector", lambda e: e.tensor_tensor(
                    dstat2[:, :], dstat2[:, :], qrb[:, :], ALU.mult))
                p.on("vector", lambda e: e.tensor_scalar(
                    dstat2[:, :], dstat2[:, :], 0.125, None, ALU.mult))
                for h in range(NH):
                    fn = lambda e, h=h: e.tensor_scalar(
                        qsc[:, h * 64:h * 64 + 64], qxb[:, h * 64:h * 64 + 64],
                        dstat2[:, h:h + 1], None, ALU.mult)
                    if h < NH - 1:
                        p.on("vector", fn)
                    else:
                        p.on("vector", lambda e, fn=fn: fn(e).then_inc(H["dsc"], 1))
                p.inc("dsc")
                p.wait("tensor", "dsc")
                p.wait("tensor", "qte", p.val("qte"))
                for hp in range(8):
                    p.on("tensor", lambda e, hp=hp: e.transpose(
                        TRN[:, hp * 64:hp * 64 + 64],
                        qsc[:, hp * 128:hp * 128 + 128], ident[0:64, 0:64]))
                p.tag("tensor", "trn")
                p.wait("vector", "trn")
                p.on("vector", lambda e: e.tensor_copy(
                    qT[:, :], TRN[:, :]).then_inc(H["qte"], 1))
                p.inc("qte")
                p.wait("tensor", "qte")
                p.wait("tensor", "ate", p.val("ate"))
                for h in range(NH):
                    sub = (h % 2) * 64
                    blk = (h // 2) * 64
                    p.on("tensor", lambda e, sub=sub, blk=blk: e.matmul(
                        S0[sub:sub + 64, blk:blk + 64],
                        snap[sub:sub + 64, blk:blk + 64],
                        qT[sub:sub + 64, blk:blk + 64],
                        start=True, stop=True))
                p.tag("tensor", "app")
                p.wait("vector", "app")
                p.on("vector", lambda e, gcol=gcol: e.tensor_copy(
                    attnT[:, :].rearrange("p (c t) -> p c t", t=2048)[:, :, gcol:gcol + 64],
                    S0[:, :].rearrange("p (c t) -> p c t", t=64)).then_inc(H["ate"], 1))
                p.inc("ate")
                for h in range(NH):
                    p.on("tensor", lambda e, rb=rb, h=h: e.matmul(
                        CTX[(h % 2) * 64:(h % 2) * 64 + 64,
                            (h // 2) * 64:(h // 2) * 64 + 64],
                        kx[rb:rb + 64, h * 64:h * 64 + 64],
                        vx[rb:rb + 64, h * 64:h * 64 + 64],
                        start=True, stop=True))
                p.tag("tensor", "app2")
                p.wait("vector", "app2")
                p.wait("vector", "kru")
                p.on("vector", lambda e: e.tensor_tensor(
                    ctx_sb[:, :], ctx_sb[:, :], CTX[:, :], ALU.add))
                p.on("vector", lambda e: e.tensor_tensor(
                    krow_f[0:1, :], krow_f[0:1, :], KROW[0:1, :], ALU.add))
                p.tag("vector", "ctu")

        p.wait("tensor", "ate")
        for j in range(4):
            res_base = p.val("res")
            p.wait("sync", "res", res_base)
            p.on("sync", lambda e, j=j: e.dma_start(
                out=xres[:, :].rearrange("p (c i t) -> p c i t", i=4, t=128),
                in_=x0_d[4 * j:4 * j + 4].rearrange("i p c t -> p c i t")).then_inc(H["ldr"], 16))
            p.inc("ldr", 16)
            for co in range(NC):
                ps = S0 if co % 2 == 0 else S1
                need = res_base - 8 + co + 7
                p.wait("tensor", "res", need)
                if j == 0 and co < 2:
                    pass
                for ci in range(NC):
                    p.on("tensor", lambda e, co=co, ci=ci, ps=ps, j=j: e.matmul(
                        ps[:, :],
                        wo_sb[:, ci * 1024 + co * 128: ci * 1024 + co * 128 + 128],
                        attnT[:, ci * 2048 + j * 512: ci * 2048 + j * 512 + 512],
                        start=(ci == 0), stop=(ci == 7)))
                p.tag("tensor", "mmo")
                p.wait("vector", "mmo")
                if co == 0:
                    p.wait("vector", "ldr")
                    p.wait("vector", "st", p.val("st"))
                p.on("vector", lambda e, co=co, ps=ps: e.tensor_tensor(
                    sT[:, co * 512:co * 512 + 512], ps[:, :],
                    xres[:, co * 512:co * 512 + 512], ALU.add).then_inc(H["res"], 1))
                p.inc("res")
            p.wait("scalar", "res")
            p.wait("scalar", "stm", p.val("stm"))
            for co in range(NC):
                p.on("scalar", lambda e, co=co: e.copy(
                    sbf[:, co * 512:co * 512 + 512], sT[:, co * 512:co * 512 + 512]))
            for co in range(NC):
                fn = lambda e, co=co: e.activation(
                    sqb[:, co * 512:co * 512 + 512], sbf[:, co * 512:co * 512 + 512],
                    AF.Square)
                if co < NC - 1:
                    p.on("scalar", fn)
                else:
                    p.on("scalar", lambda e, fn=fn: fn(e).then_inc(H["sbc"], 1))
            p.inc("sbc")
            p.wait("tensor", "sbc")
            p.wait("tensor", "mrw", p.val("mrw"))
            for co in range(NC):
                p.on("tensor", lambda e, co=co: e.matmul(
                    KROW[0:1, 0:512], onesb[:, 0:1],
                    sbf[:, co * 512:co * 512 + 512], start=(co == 0), stop=(co == 7)))
            for co in range(NC):
                p.on("tensor", lambda e, co=co: e.matmul(
                    KROW[0:1, 512:1024], onesb[:, 0:1],
                    sqb[:, co * 512:co * 512 + 512], start=(co == 0), stop=(co == 7)))
            p.tag("tensor", "stm")
            p.wait("vector", "stm")
            p.on("vector", lambda e: e.tensor_scalar(
                rowm[0:1, :], KROW[0:1, 0:512], 1.0 / D, None, ALU.mult))
            p.on("vector", lambda e: e.tensor_scalar(
                rowv[0:1, :], KROW[0:1, 512:1024], 1.0 / D, EPS_OUT, ALU.mult, ALU.add))
            p.on("vector", lambda e: e.tensor_tensor(
                rowmr[0:1, :], rowm[0:1, :], rowm[0:1, :], ALU.mult))
            p.on("vector", lambda e: e.tensor_tensor(
                rowv[0:1, :], rowv[0:1, :], rowmr[0:1, :],
                ALU.subtract).then_inc(H["var"], 1))
            p.inc("var")
            p.wait("scalar", "var")
            p.on("scalar", lambda e: e.activation(
                rowr[0:1, :], rowv[0:1, :], AF.Sqrt).then_inc(H["rsr"], 1))
            p.inc("rsr")
            p.wait("vector", "rsr")
            p.on("vector", lambda e: e.reciprocal(rowr[0:1, :], rowr[0:1, :]))
            p.on("vector", lambda e: e.tensor_tensor(
                rowmr[0:1, :], rowm[0:1, :], rowr[0:1, :],
                ALU.mult).then_inc(H["mrw"], 1))
            p.inc("mrw")
            p.wait("tensor", "mrw")
            p.wait("tensor", "nrm", p.val("nrm"))
            p.on("tensor", lambda e: e.matmul(
                S2[:, :], onesf[0:1, :], rowr[0:1, :], start=True, stop=True))
            p.on("tensor", lambda e: e.matmul(
                S3[:, :], onesf[0:1, :], rowmr[0:1, :], start=True, stop=True))
            p.tag("tensor", "rep")
            p.wait("vector", "rep")
            for co in range(NC):
                p.on("vector", lambda e, co=co: e.tensor_tensor(
                    sT[:, co * 512:co * 512 + 512], sT[:, co * 512:co * 512 + 512],
                    S2[:, :], ALU.mult))
                fn = lambda e, co=co: e.tensor_tensor(
                    sT[:, co * 512:co * 512 + 512], sT[:, co * 512:co * 512 + 512],
                    S3[:, :], ALU.subtract)
                if co < NC - 1:
                    p.on("vector", fn)
                else:
                    p.on("vector", lambda e, fn=fn: fn(e).then_inc(H["nrm"], 1))
            p.inc("nrm")
            p.wait("sync", "nrm")
            p.on("sync", lambda e, j=j: e.dma_start(
                out=x1_d[4 * j:4 * j + 4].rearrange("i p c t -> p c i t"),
                in_=sT[:, :].rearrange("p (c i t) -> p c i t", i=4, t=128)).then_inc(H["st"], 16))
            p.inc("st", 16)

        p.wait("sync", "st")
        p.simulate()
        p.emit(ctx)
    return nc



def _st_layout(xT):
    return np.ascontiguousarray(xT.reshape(NC, 128, NT, 128).transpose(2, 1, 0, 3))


def _unst_layout(xst):
    return np.ascontiguousarray(np.asarray(xst).transpose(2, 1, 0, 3).reshape(D, T))


def prep_l1_maps(inputs):
    x = np.asarray(inputs["L0_attn_x"], dtype=np.float32)
    wq = np.asarray(inputs["L0_attn_W_qkv"], dtype=np.float32)
    wo = np.asarray(inputs["L0_attn_W_out"], dtype=np.float32)
    wqT = np.ascontiguousarray(wq.T).reshape(NC, 128, 3072)
    woT = np.ascontiguousarray(wo.T).reshape(NC, 128, D)
    ident = np.eye(128, dtype=BF)
    ones = np.ones((128, 128), dtype=BF)
    onesf = np.ones((1, 128), dtype=np.float32)
    zeros_st = np.zeros((NT, 128, NC, 128), dtype=np.float32)
    ind1 = np.ones((128, 1), dtype=np.float32)
    ind0 = np.zeros((128, 1), dtype=np.float32)
    maps = []
    for core in range(NCORES):
        b, h = core // 2, core % 2
        x0st = _st_layout(np.ascontiguousarray(x[b, h * T:(h + 1) * T].T))
        if h == 1:
            xpst = _st_layout(np.ascontiguousarray(x[b, 0:T].T))
            indv = ind1
        else:
            xpst = zeros_st
            indv = ind0
        maps.append(dict(x0st=x0st, xpst=xpst, wqkvT=wqT, woutT=woT, ind=indv,
                         identbf=ident, onesbf=ones, onesf=onesf))
    return maps


def kernel(**inputs):
    l1_maps = prep_l1_maps(inputs)
    nc1 = build_l1()
    r1 = run_bass_kernel_spmd(nc1, l1_maps, core_ids=list(range(NCORES)))
    x1sts = [np.ascontiguousarray(np.asarray(r1.results[c]["x1st"], dtype=np.float32))
             for c in range(NCORES)]

    sh2 = prep_l2_shared(inputs)
    ident = np.eye(128, dtype=BF)
    l2_maps = []
    for c in range(NCORES):
        m = dict(sh2)
        m.update(x1st=x1sts[c], identbf=ident)
        l2_maps.append(m)
    nc2 = build_l2()
    r2 = run_bass_kernel_spmd(nc2, l2_maps, core_ids=list(range(NCORES)))
    x2sts = [np.ascontiguousarray(np.asarray(r2.results[c]["x2st"], dtype=np.float32))
             for c in range(NCORES)]

    sh3 = prep_l3_shared(inputs)
    zst = np.zeros((NT, 128, NC, 128), dtype=np.float32)
    l3_maps = []
    for c in range(NCORES):
        m = dict(sh3)
        m.update(x2st=x2sts[c],
                 pxst=x2sts[c - 1] if c % 2 == 1 else zst,
                 **prep_l3_core(c))
        l3_maps.append(m)
    nc3 = build_l3()
    r3 = run_bass_kernel_spmd(nc3, l3_maps, core_ids=list(range(NCORES)))
    x3sts = [np.ascontiguousarray(np.asarray(r3.results[c]["x3st"], dtype=np.float32))
             for c in range(NCORES)]

    sh4 = prep_l4_shared(inputs)
    l4_maps = []
    for c in range(NCORES):
        m = dict(sh4)
        m.update(x3st=x3sts[c])
        l4_maps.append(m)
    nc4 = build_l4()
    r4 = run_bass_kernel_spmd(nc4, l4_maps, core_ids=list(range(NCORES)))

    out = np.zeros((4, 4096, D), dtype=np.float32)
    for core in range(NCORES):
        b, h = core // 2, core % 2
        out[b, h * T:(h + 1) * T] = _unst_layout(r4.results[core]["x4st"]).T
    return out



NE = 16
JC = 16
TT2 = 256
NT2 = T // TT2


def build_l2():
    nc = bass.Bass("TRN2", target_bir_lowering=False)
    x1_d = nc.dram_tensor("x1st", [NT, 128, NC, 128], F32, kind="ExternalInput").ap()
    w1_d = nc.dram_tensor("w1", [NE, D, 2048], F32, kind="ExternalInput").ap()
    w2_d = nc.dram_tensor("w2", [NE, 2048, D], F32, kind="ExternalInput").ap()
    wg_d = nc.dram_tensor("wgc", [128, NC, NE], F32, kind="ExternalInput").ap()
    bias2_d = nc.dram_tensor("bias2", [2, NE], F32, kind="ExternalInput").ap()
    b1c_d = nc.dram_tensor("b1c", [128, NE * JC], F32, kind="ExternalInput").ap()
    b2b_d = nc.dram_tensor("b2b", [NE, D], BF16, kind="ExternalInput").ap()
    esel_d = nc.dram_tensor("esel", [NE, NE * 128], BF16, kind="ExternalInput").ap()
    ones_d = nc.dram_tensor("onesbf", [128, 128], BF16, kind="ExternalInput").ap()
    ident_d = nc.dram_tensor("identbf", [128, 128], BF16, kind="ExternalInput").ap()
    onesf_d = nc.dram_tensor("onesf", [2, 128], F32, kind="ExternalInput").ap()
    x2_d = nc.dram_tensor("x2st", [NT, 128, NC, 128], F32, kind="ExternalOutput").ap()

    with ExitStack() as ctx:
        sb = lambda n, s, d=BF16: ctx.enter_context(nc.sbuf_tensor(n, s, d))
        psa = lambda n, s, d=F32: ctx.enter_context(nc.psum_tensor(n, s, d))

        OUTP = psa("OUTP", [128, 1024])
        H2 = psa("H2", [128, 1024])
        CWREP = psa("CWREP", [128, 512])
        PSM = psa("PSM", [128, 512])
        REP = psa("REP", [128, 512])
        TRB = psa("TRB", [16, 512], BF16)

        xt2 = [sb(f"xt2_{j}", [128, NC * TT2]) for j in range(2)]
        out_sb = sb("out_sb", [128, NC * 2048], F32)
        wring = [sb(f"wr{s}", [128, 8 * 1024]) for s in range(4)]
        hs = [sb(f"hs{s}", [128, JC * TT2]) for s in range(1)]
        cwrep_sb = sb("cwrep_sb", [128, TT2])
        cw_all = sb("cw_all", [128, 2 * NT2 * NE], F32)
        cwT_all = sb("cwT_all", [16, T])
        lg_sb = sb("lg_sb", [128, 16], F32)
        top8 = sb("top8", [128, 8], F32)
        gsc = sb("gsc", [128, 8], F32)
        eqa = sb("eqa", [128, 16], F32)
        eqb = sb("eqb", [128, 16], F32)
        wg_sb = sb("wg_sb", [128, NC * NE], F32)
        bias2_sb = sb("bias2_sb", [2, NE], F32)
        b1c = sb("b1c_sb", [128, NE * JC], F32)
        b2b = sb("b2b_sb", [NE, D])
        esel = sb("esel_sb", [NE, NE * 128])
        xres = sb("xres2", [128, NC * TT2], F32)
        sT = sb("sT2", [128, NC * TT2], F32)
        sbf = sb("sbf2", [128, NC * TT2])
        sqb = sb("sqb2", [128, NC * TT2])
        rowm = sb("rowm2", [1, TT2], F32)
        rowv = sb("rowv2", [1, TT2], F32)
        rowr = sb("rowr2", [1, TT2], F32)
        rowmr = sb("rowmr2", [1, TT2], F32)
        onesb = sb("onesb2", [128, 128])
        ident = sb("ident2", [128, 128])
        onesf = sb("onesf2", [2, 128], F32)
        zrow = sb("zrow", [1, 512])

        p = Plan(nc)
        p.mksem(ctx, "_sf_vector", "_sf_scalar", "ldc", "xl0", "xl1", "ldr",
                "wl0", "wl1", "wl2", "wl3",
                "gmm", "gcw", "cwr", "cwe", "hmd", "hev", "hsd",
                "odc", "oad", "b2m", "mmo", "res", "sbc", "stm", "var",
                "rsr", "mrw", "rep", "nrm", "st")
        H = p.handles

        p.on("sync", lambda e: e.dma_start(out=wg_sb[:, :].rearrange(
            "p (c n) -> p c n", c=NC), in_=wg_d[:, :, :]).then_inc(H["ldc"], 16))
        p.inc("ldc", 16)
        for dst, src in ((bias2_sb, bias2_d), (b1c, b1c_d), (b2b, b2b_d),
                         (esel, esel_d), (onesb, ones_d), (ident, ident_d),
                         (onesf, onesf_d)):
            p.on("sync", lambda e, d=dst, s=src: e.dma_start(
                out=d[:, :], in_=s[:, :]).then_inc(H["ldc"], 16))
            p.inc("ldc", 16)
        for eng in ("tensor", "scalar", "vector"):
            p.wait(eng, "ldc")
        p.on("vector", lambda e: e.memset(zrow[:, :], 0.0))

        for tt in range(NT2):
            p.wait("sync", "gcw", p.val("gcw"))
            p.on("sync", lambda e, tt=tt: e.dma_start(
                out=xres[:, :].rearrange("p (c i t) -> p c i t", i=2, t=128),
                in_=x1_d[2 * tt:2 * tt + 2].rearrange(
                    "i p c t -> p c i t")).then_inc(H["ldr"], 16))
            p.inc("ldr", 16)
            p.wait("tensor", "ldr")
            for s2 in range(2):
                ga = PSM[:, s2 * 16:s2 * 16 + 16]
                p.wait("tensor", "gcw", p.val("gcw") - 1)
                for c in range(NC):
                    p.on("tensor", lambda e, c=c, s2=s2, ga=ga: e.matmul(
                        ga, xres[:, c * 256 + s2 * 128: c * 256 + s2 * 128 + 128],
                        wg_sb[:, c * 16:c * 16 + 16], start=(c == 0), stop=False))
                p.on("tensor", lambda e, ga=ga: e.matmul(
                    ga, onesf[:, :], bias2_sb[:, :], start=False, stop=True))
                p.tag("tensor", "gmm")
                p.wait("vector", "gmm")
                p.on("vector", lambda e, ga=ga: e.tensor_copy(lg_sb[:, :], ga))
                p.on("vector", lambda e: e.max(out=top8[:, :], in_=lg_sb[:, :]))
                p.on("vector", lambda e: e.tensor_tensor(
                    gsc[:, 0:1], top8[:, 1:2], top8[:, 0:1], ALU.subtract))
                p.tag("vector", "gcw")
                p.wait("scalar", "gcw")
                p.on("scalar", lambda e: e.activation(
                    gsc[:, 1:2], gsc[:, 0:1], AF.Exp).then_inc(H["gcw"], 1))
                p.inc("gcw")
                p.wait("vector", "gcw")
                p.on("vector", lambda e: e.tensor_scalar(
                    gsc[:, 2:3], gsc[:, 1:2], 1.0, None, ALU.add))
                p.on("vector", lambda e: e.reciprocal(gsc[:, 3:4], gsc[:, 2:3]))
                p.on("vector", lambda e: e.tensor_tensor(
                    gsc[:, 4:5], gsc[:, 1:2], gsc[:, 3:4], ALU.mult))
                p.on("vector", lambda e: e.tensor_scalar(
                    eqa[:, :], lg_sb[:, :], top8[:, 0:1], None, ALU.is_equal))
                p.on("vector", lambda e: e.tensor_scalar(
                    eqb[:, :], lg_sb[:, :], top8[:, 1:2], None, ALU.is_equal))
                p.on("vector", lambda e: e.tensor_scalar(
                    eqa[:, :], eqa[:, :], gsc[:, 3:4], None, ALU.mult))
                p.on("vector", lambda e: e.tensor_scalar(
                    eqb[:, :], eqb[:, :], gsc[:, 4:5], None, ALU.mult))
                col = (2 * tt + s2) * 16
                p.on("vector", lambda e, col=col: e.tensor_tensor(
                    cw_all[:, col:col + 16], eqa[:, :], eqb[:, :], ALU.add))
                p.tag("vector", "gcw")
                p.wait("scalar", "gcw")
                p.wait("scalar", "gmm", p.val("gmm"))
                p.on("scalar", lambda e, col=col: e.copy(
                    sbf[:, 0:16], cw_all[:, col:col + 16]).then_inc(H["gcw"], 1))
                p.inc("gcw")
                p.wait("tensor", "gcw")
                tcol = (2 * tt + s2) * 128
                p.on("tensor", lambda e, s2=s2: e.transpose(
                    TRB[:, s2 * 128:s2 * 128 + 128], sbf[:, 0:16], ident[:, :]))
                p.tag("tensor", "gmm")
                p.wait("vector", "gmm")
                p.on("vector", lambda e, tcol=tcol, s2=s2: e.tensor_copy(
                    cwT_all[:, tcol:tcol + 128],
                    TRB[:, s2 * 128:s2 * 128 + 128]).then_inc(H["gcw"], 1))
                p.inc("gcw")
        gcw_done = p.val("gcw")
        p.wait("tensor", "gcw", gcw_done)

        def wslice(part, jc, dc=None):
            buf = wring[part]
            if part < 2:
                return lambda c, jc=jc: buf[:, c * 1024 + (jc % 8) * 128:
                                            c * 1024 + (jc % 8) * 128 + 128]
            return buf[:, (jc % 8) * 1024 + dc * 128:(jc % 8) * 1024 + dc * 128 + 128]

        for e_i in range(NE):
            for part in range(4):
                if e_i > 0:
                    if part < 2:
                        p.wait("gpsimd", "hmd", p.val("hmd"))
                    else:
                        p.wait("gpsimd", "odc", p.val("odc"))
                if part == 0:
                    src = lambda ee: w1_d[ee].rearrange("(c p) n -> p c n", p=128)[:, :, 0:1024]
                elif part == 1:
                    src = lambda ee: w1_d[ee].rearrange("(c p) n -> p c n", p=128)[:, :, 1024:2048]
                elif part == 2:
                    src = lambda ee: w2_d[ee, 0:1024].rearrange("(j p) n -> p j n", p=128)
                else:
                    src = lambda ee: w2_d[ee, 1024:2048].rearrange("(j p) n -> p j n", p=128)
                p.on("gpsimd", lambda e, part=part, src=src, ee=e_i: e.dma_start(
                    out=wring[part][:, :].rearrange("p (a n) -> p a n", a=8),
                    in_=src(ee)).then_inc(H[f"wl{part}"], 16))
                p.inc(f"wl{part}", 16)

            for tt in range(NT2):
                hbuf = hs[0]
                xbuf = xt2[tt % 2]
                xsem = "xl0" if tt % 2 == 0 else "xl1"
                p.wait("gpsimd", "hmd", p.val("hmd") - 16)
                p.on("gpsimd", lambda e, tt=tt, b=xbuf, xs=xsem: e.dma_start(
                    out=b[:, :].rearrange("p (c i t) -> p c i t", i=2, t=128),
                    in_=x1_d[2 * tt:2 * tt + 2].rearrange(
                        "i p c t -> p c i t")).then_inc(H[xs], 16))
                p.inc(xsem, 16)
                p.wait("tensor", "cwe", p.val("cwe"))
                p.on("tensor", lambda e, e_i=e_i, tt=tt: e.matmul(
                    CWREP[:, 0:TT2], esel[:, e_i * 128:e_i * 128 + 128],
                    cwT_all[:, tt * TT2:tt * TT2 + TT2], start=True, stop=True))
                p.tag("tensor", "cwr")
                p.wait("vector", "cwr")
                p.wait("vector", "hsd", p.val("hsd"))
                p.on("vector", lambda e: e.tensor_copy(
                    cwrep_sb[:, :], CWREP[:, 0:TT2]).then_inc(H["cwe"], 1))
                p.inc("cwe")
                p.wait("tensor", xsem)
                p.wait("tensor", "wl0", 16 * (e_i + 1))
                p.wait("tensor", "wl1", 16 * (e_i + 1))
                if p.val("hev") >= 2:
                    p.wait("tensor", "hev", p.val("hev") - 2)
                for jc in range(JC):
                    reg = H2[:, 0:TT2] if jc % 2 == 0 else H2[:, 512:512 + TT2]
                    if jc >= 2:
                        p.wait("tensor", "hev", p.val("hev") - 1)
                    w1s = wslice(jc // 8, jc)
                    for c in range(NC):
                        p.on("tensor", lambda e, c=c, jc=jc, reg=reg, w1s=w1s, xb=xbuf: e.matmul(
                            reg, w1s(c), xb[:, c * TT2: c * TT2 + TT2],
                            start=(c == 0), stop=(c == 7)))
                    p.tag("tensor", "hmd")
                    p.wait("scalar", "hmd")
                    p.on("scalar", lambda e, jc=jc, reg=reg, e_i=e_i, hbuf=hbuf: e.activation(
                        hbuf[:, jc * TT2:jc * TT2 + TT2], reg, AF.Relu,
                        bias=b1c[:, e_i * JC + jc:e_i * JC + jc + 1]).then_inc(H["hev"], 1))
                    p.inc("hev")
                p.wait("vector", "hev")
                p.wait("vector", "cwe")
                for jc in range(JC):
                    fn = lambda e, jc=jc, hbuf=hbuf: e.tensor_tensor(
                        hbuf[:, jc * TT2:jc * TT2 + TT2],
                        hbuf[:, jc * TT2:jc * TT2 + TT2], cwrep_sb[:, :], ALU.mult)
                    if jc < JC - 1:
                        p.on("vector", fn)
                    else:
                        p.on("vector", lambda e, fn=fn: fn(e).then_inc(H["hsd"], 1))
                p.inc("hsd")
                p.wait("tensor", "hsd")
                p.wait("tensor", "wl2", 16 * (e_i + 1))
                p.wait("tensor", "wl3", 16 * (e_i + 1))
                for dc in range(NC):
                    reg = OUTP[:, 0:TT2] if dc % 2 == 0 else OUTP[:, 512:512 + TT2]
                    if p.val("oad") >= 2:
                        p.wait("tensor", "oad", p.val("oad") - 1)
                    for jc in range(JC):
                        part = 2 + jc // 8
                        p.on("tensor", lambda e, jc=jc, dc=dc, reg=reg, part=part, hbuf=hbuf: e.matmul(
                            reg, wslice(part, jc, dc), hbuf[:, jc * TT2:jc * TT2 + TT2],
                            start=(jc == 0), stop=(jc == JC - 1)))
                    p.tag("tensor", "odc")
                    p.wait("vector", "odc")
                    col = dc * 2048 + tt * TT2
                    if e_i == 0:
                        fn = lambda e, col=col, reg=reg: e.tensor_copy(
                            out_sb[:, col:col + TT2], reg)
                    else:
                        fn = lambda e, col=col, reg=reg: e.tensor_tensor(
                            out_sb[:, col:col + TT2], out_sb[:, col:col + TT2],
                            reg, ALU.add)
                    p.on("vector", fn)
                    p.tag("vector", "oad")

        for tt in range(NT2):
            for dc in range(NC):
                reg = OUTP[:, 0:TT2] if dc % 2 == 0 else OUTP[:, 512:512 + TT2]
                p.wait("tensor", "oad", p.val("oad") - 1)
                p.on("tensor", lambda e, dc=dc, tt=tt, reg=reg: e.matmul(
                    reg, b2b[:, dc * 128:dc * 128 + 128],
                    cwT_all[:, tt * TT2:tt * TT2 + TT2], start=True, stop=True))
                p.tag("tensor", "odc")
                p.wait("vector", "odc")
                col = dc * 2048 + tt * TT2
                p.on("vector", lambda e, col=col, reg=reg: e.tensor_tensor(
                    out_sb[:, col:col + TT2], out_sb[:, col:col + TT2], reg, ALU.add))
                p.tag("vector", "oad")

        p.wait("sync", "gcw", gcw_done)
        for tt in range(NT2):
            res_base = p.val("res")
            p.wait("sync", "res", res_base)
            p.on("sync", lambda e, tt=tt: e.dma_start(
                out=xres[:, :].rearrange("p (c i t) -> p c i t", i=2, t=128),
                in_=x1_d[2 * tt:2 * tt + 2].rearrange(
                    "i p c t -> p c i t")).then_inc(H["ldr"], 16))
            p.inc("ldr", 16)
            p.wait("vector", "ldr")
            p.wait("vector", "oad")
            p.wait("vector", "st", p.val("st"))
            for co in range(NC):
                fn = lambda e, co=co, tt=tt: e.tensor_tensor(
                    sT[:, co * TT2:co * TT2 + TT2],
                    out_sb[:, co * 2048 + tt * TT2: co * 2048 + tt * TT2 + TT2],
                    xres[:, co * TT2:co * TT2 + TT2], ALU.add)
                if co < NC - 1:
                    p.on("vector", fn)
                else:
                    p.on("vector", lambda e, fn=fn: fn(e).then_inc(H["res"], 1))
            p.inc("res")
            p.wait("scalar", "res")
            p.wait("scalar", "stm", p.val("stm"))
            for co in range(NC):
                p.on("scalar", lambda e, co=co: e.copy(
                    sbf[:, co * TT2:co * TT2 + TT2], sT[:, co * TT2:co * TT2 + TT2]))
            for co in range(NC):
                fn = lambda e, co=co: e.activation(
                    sqb[:, co * TT2:co * TT2 + TT2], sbf[:, co * TT2:co * TT2 + TT2],
                    AF.Square)
                if co < NC - 1:
                    p.on("scalar", fn)
                else:
                    p.on("scalar", lambda e, fn=fn: fn(e).then_inc(H["sbc"], 1))
            p.inc("sbc")
            p.wait("tensor", "sbc")
            p.wait("tensor", "mrw", p.val("mrw"))
            for co in range(NC):
                p.on("tensor", lambda e, co=co: e.matmul(
                    PSM[0:1, 0:TT2], onesb[:, 0:1],
                    sbf[:, co * TT2:co * TT2 + TT2], start=(co == 0), stop=(co == 7)))
            for co in range(NC):
                p.on("tensor", lambda e, co=co: e.matmul(
                    PSM[0:1, TT2:2 * TT2], onesb[:, 0:1],
                    sqb[:, co * TT2:co * TT2 + TT2], start=(co == 0), stop=(co == 7)))
            p.tag("tensor", "stm")
            p.wait("vector", "stm")
            p.on("vector", lambda e: e.tensor_scalar(
                rowm[0:1, :], PSM[0:1, 0:TT2], 1.0 / D, None, ALU.mult))
            p.on("vector", lambda e: e.tensor_scalar(
                rowv[0:1, :], PSM[0:1, TT2:2 * TT2], 1.0 / D, EPS_OUT, ALU.mult, ALU.add))
            p.on("vector", lambda e: e.tensor_tensor(
                rowmr[0:1, :], rowm[0:1, :], rowm[0:1, :], ALU.mult))
            p.on("vector", lambda e: e.tensor_tensor(
                rowv[0:1, :], rowv[0:1, :], rowmr[0:1, :],
                ALU.subtract).then_inc(H["var"], 1))
            p.inc("var")
            p.wait("scalar", "var")
            p.on("scalar", lambda e: e.activation(
                rowr[0:1, :], rowv[0:1, :], AF.Sqrt).then_inc(H["rsr"], 1))
            p.inc("rsr")
            p.wait("vector", "rsr")
            p.on("vector", lambda e: e.reciprocal(rowr[0:1, :], rowr[0:1, :]))
            p.on("vector", lambda e: e.tensor_tensor(
                rowmr[0:1, :], rowm[0:1, :], rowr[0:1, :],
                ALU.mult).then_inc(H["mrw"], 1))
            p.inc("mrw")
            p.wait("tensor", "mrw")
            p.wait("tensor", "nrm", p.val("nrm"))
            p.on("tensor", lambda e: e.matmul(
                REP[:, 0:TT2], onesf[0:1, :], rowr[0:1, :], start=True, stop=True))
            p.on("tensor", lambda e: e.matmul(
                REP[:, TT2:2 * TT2], onesf[0:1, :], rowmr[0:1, :], start=True, stop=True))
            p.tag("tensor", "rep")
            p.wait("vector", "rep")
            for co in range(NC):
                p.on("vector", lambda e, co=co: e.tensor_tensor(
                    sT[:, co * TT2:co * TT2 + TT2], sT[:, co * TT2:co * TT2 + TT2],
                    REP[:, 0:TT2], ALU.mult))
                fn = lambda e, co=co: e.tensor_tensor(
                    sT[:, co * TT2:co * TT2 + TT2], sT[:, co * TT2:co * TT2 + TT2],
                    REP[:, TT2:2 * TT2], ALU.subtract)
                if co < NC - 1:
                    p.on("vector", fn)
                else:
                    p.on("vector", lambda e, fn=fn: fn(e).then_inc(H["nrm"], 1))
            p.inc("nrm")
            p.wait("sync", "nrm")
            p.on("sync", lambda e, tt=tt: e.dma_start(
                out=x2_d[2 * tt:2 * tt + 2].rearrange("i p c t -> p c i t"),
                in_=sT[:, :].rearrange("p (c i t) -> p c i t", i=2, t=128)).then_inc(H["st"], 16))
            p.inc("st", 16)

        p.wait("sync", "st")
        p.simulate()
        p.emit(ctx)
    return nc


def prep_l2_shared(inputs):
    wg = np.asarray(inputs["L0_ffn_Wg"], dtype=np.float32)
    bg = np.asarray(inputs["L0_ffn_bg"], dtype=np.float32)
    bias = np.asarray(inputs["L0_ffn_bias"], dtype=np.float32)
    w1 = np.ascontiguousarray(np.asarray(inputs["L0_ffn_W1"], dtype=np.float32))
    b1 = np.asarray(inputs["L0_ffn_b1"], dtype=np.float32)
    w2 = np.ascontiguousarray(np.asarray(inputs["L0_ffn_W2"], dtype=np.float32))
    b2 = np.asarray(inputs["L0_ffn_b2"], dtype=np.float32)
    wgc = np.ascontiguousarray(wg.reshape(NC, 128, NE).transpose(1, 0, 2))
    bias2 = np.ascontiguousarray(np.stack([bg, bias]))
    b1c = np.ascontiguousarray(b1.reshape(NE, JC, 128).transpose(2, 0, 1)
                               ).reshape(128, NE * JC)
    b2b = b2.astype(BF)
    esel = np.zeros((NE, NE * 128), dtype=BF)
    for ee in range(NE):
        esel[ee, ee * 128:(ee + 1) * 128] = 1
    ones = np.ones((128, 128), dtype=BF)
    onesf2 = np.ones((2, 128), dtype=np.float32)
    return dict(w1=w1, w2=w2, wgc=wgc, bias2=bias2, b1c=b1c, b2b=b2b,
                esel=esel, onesbf=ones, onesf=onesf2)



KT = 31
CE = T + 30


def emit_ln3(p, H, env, eps, affine, W=512):
    sT, sbf, sqb = env["sT"], env["sbf"], env["sqb"]
    rowm, rowv, rowr, rowmr = env["rowm"], env["rowv"], env["rowr"], env["rowmr"]
    LROW, LROW2 = env["LROW"], env["LROW2"]
    REPv, REPm, onesb, onesf, gb = env["REPv"], env["REPm"], env["onesb"], env["onesf"], env["gb"]
    p.wait("scalar", "res")
    p.wait("scalar", "stm", p.val("stm"))
    for co in range(NC):
        p.on("scalar", lambda e, co=co: e.copy(
            sbf[:, co * W:co * W + W], sT[:, co * W:co * W + W]))
    for co in range(NC):
        p.on("scalar", lambda e, co=co: e.activation(
            sqb[:, co * W:co * W + W], sbf[:, co * W:co * W + W], AF.Square))
    p.tag("scalar", "sbc")
    p.wait("tensor", "sbc")
    p.wait("tensor", "mrw", p.val("mrw"))
    for co in range(NC):
        p.on("tensor", lambda e, co=co: e.matmul(
            LROW[0:1, 0:W], onesb[:, 0:1], sbf[:, co * W:co * W + W],
            start=(co == 0), stop=(co == NC - 1)))
    for co in range(NC):
        p.on("tensor", lambda e, co=co: e.matmul(
            LROW2[0:1, 0:W], onesb[:, 0:1], sqb[:, co * W:co * W + W],
            start=(co == 0), stop=(co == NC - 1)))
    p.tag("tensor", "stm")
    p.wait("vector", "stm")
    p.on("vector", lambda e: e.tensor_scalar(
        rowm[0:1, 0:W], LROW[0:1, 0:W], 1.0 / D, None, ALU.mult))
    p.on("vector", lambda e, eps=eps: e.tensor_scalar(
        rowv[0:1, 0:W], LROW2[0:1, 0:W], 1.0 / D, eps, ALU.mult, ALU.add))
    p.on("vector", lambda e: e.tensor_tensor(
        rowmr[0:1, 0:W], rowm[0:1, 0:W], rowm[0:1, 0:W], ALU.mult))
    p.on("vector", lambda e: e.tensor_tensor(
        rowv[0:1, 0:W], rowv[0:1, 0:W], rowmr[0:1, 0:W], ALU.subtract))
    p.tag("vector", "var")
    p.wait("scalar", "var")
    p.on("scalar", lambda e: e.activation(rowr[0:1, 0:W], rowv[0:1, 0:W], AF.Sqrt))
    p.tag("scalar", "rsr")
    p.wait("vector", "rsr")
    p.on("vector", lambda e: e.reciprocal(rowr[0:1, 0:W], rowr[0:1, 0:W]))
    p.on("vector", lambda e: e.tensor_tensor(
        rowmr[0:1, 0:W], rowm[0:1, 0:W], rowr[0:1, 0:W], ALU.mult))
    p.tag("vector", "mrw")
    p.wait("tensor", "mrw")
    p.wait("tensor", "nrm", p.val("nrm"))
    for b in range(max(1, W // 256)):
        p.on("tensor", lambda e, b=b: e.matmul(
            REPv[:, b * 256:b * 256 + min(256, W)], onesf[0:1, :],
            rowr[0:1, b * 256:b * 256 + min(256, W)], start=True, stop=True))
        p.on("tensor", lambda e, b=b: e.matmul(
            REPm[:, b * 256:b * 256 + min(256, W)], onesf[0:1, :],
            rowmr[0:1, b * 256:b * 256 + min(256, W)], start=True, stop=True))
    p.tag("tensor", "rep")
    p.wait("vector", "rep")
    for co in range(NC):
        p.on("vector", lambda e, co=co: e.tensor_tensor(
            sT[:, co * W:co * W + W], sT[:, co * W:co * W + W],
            REPv[:, 0:W], ALU.mult))
        p.on("vector", lambda e, co=co: e.tensor_tensor(
            sT[:, co * W:co * W + W], sT[:, co * W:co * W + W],
            REPm[:, 0:W], ALU.subtract))
        if affine:
            p.on("vector", lambda e, co=co: e.tensor_scalar(
                sT[:, co * W:co * W + W], sT[:, co * W:co * W + W],
                gb[:, co:co + 1], gb[:, NC + co:NC + co + 1], ALU.mult, ALU.add))
    p.tag("vector", "nrm")


def build_l3():
    nc = bass.Bass("TRN2", target_bir_lowering=False)
    x2_d = nc.dram_tensor("x2st", [NT, 128, NC, 128], F32, kind="ExternalInput").ap()
    px_d = nc.dram_tensor("pxst", [NT, 128, NC, 128], F32, kind="ExternalInput").ap()
    wpe_d = nc.dram_tensor("wpe", [128, 62], BF16, kind="ExternalInput").ap()
    bpe_d = nc.dram_tensor("bpe", [1, 62], BF16, kind="ExternalInput").ap()
    sttr_d = nc.dram_tensor("sttrep", [128, 496], F32, kind="ExternalInput").ap()
    mask_d = nc.dram_tensor("maskrep", [128, 496], BF16, kind="ExternalInput").ap()
    inv_d = nc.dram_tensor("invpos", [128, T], BF16, kind="ExternalInput").ap()
    invl_d = nc.dram_tensor("invlead", [128, 30], F32, kind="ExternalInput").ap()
    e4_d = nc.dram_tensor("e4sel", [124, 62 * 128], BF16, kind="ExternalInput").ap()
    r2_d = nc.dram_tensor("r2sel", [16, NC * 128], BF16, kind="ExternalInput").ap()
    wfc_d = nc.dram_tensor("wfcT", [NC, 128, D], F32, kind="ExternalInput").ap()
    gb_d = nc.dram_tensor("gbcols", [128, 2 * NC], F32, kind="ExternalInput").ap()
    ind_d = nc.dram_tensor("ind", [128, 1], F32, kind="ExternalInput").ap()
    ident_d = nc.dram_tensor("identbf", [128, 128], BF16, kind="ExternalInput").ap()
    ones_d = nc.dram_tensor("onesbf", [128, 128], BF16, kind="ExternalInput").ap()
    onesf_d = nc.dram_tensor("onesf", [1, 128], F32, kind="ExternalInput").ap()
    x3_d = nc.dram_tensor("x3st", [NT, 128, NC, 128], F32, kind="ExternalOutput").ap()

    with ExitStack() as ctx:
        sb = lambda n, s, d=BF16: ctx.enter_context(nc.sbuf_tensor(n, s, d))
        psa = lambda n, s, d=F32: ctx.enter_context(nc.psum_tensor(n, s, d))

        PC = psa("PC", [128, 496])
        TRQ = psa("TRQ", [124, 1024], BF16)
        WREP = psa("WREP", [128, 512])
        WREP2 = psa("WREP2", [128, 512])
        WFCa = psa("WFCa", [128, 512])
        LROW = psa("LROW", [1, 512])
        LROW2 = psa("LROW2", [1, 512])
        REPv = WFCa
        REPm = WREP

        ci_ext = sb("ci_ext", [128, NC * CE])
        clT = sb("clT", [128, NC * T])
        wq4 = [sb(f"wq4_{g}", [124, T]) for g in range(4)]
        rcpT = sb("rcpT", [16, T])
        xc = sb("xc", [128, T], F32)
        cums = sb("cums", [128, T], F32)
        onessc = sb("onessc", [128, T])
        ptail = sb("ptail", [128, NC * 30], F32)
        invpos = sb("invpos_sb", [128, T])
        invl = sb("invl_sb", [128, 30], F32)
        sttr = sb("sttr_sb", [128, 496], F32)
        maskr = sb("maskr_sb", [128, 496])
        sg = sb("sg_sb", [128, 496], F32)
        dyn = sb("dyn_sb", [128, 496], F32)
        wexp = sb("wexp_sb", [128, 496])
        wsum = sb("wsum", [128, 16], F32)
        rcp = sb("rcp", [128, 16], F32)
        rcpb = sb("rcpb", [128, 16])
        wpe = sb("wpe_sb", [128, 62])
        bpe = sb("bpe_sb", [1, 62])
        e4 = sb("e4_sb", [124, 62 * 128])
        r2 = sb("r2_sb", [16, NC * 128])
        wrepb = sb("wrepb", [128, T])
        prodb = sb("prodb", [128, T])
        cacc = sb("cacc", [128, T], F32)
        wfc = sb("wfc_sb", [128, NC * D])
        gb = sb("gb_sb", [128, 2 * NC], F32)
        xres = sb("xres3", [128, NC * 256], F32)
        sT = sb("sT3", [128, NC * 256], F32)
        sbf = sb("sbf3", [128, NC * 256])
        sqb = sb("sqb3", [128, NC * 256])
        rowm = sb("rowm3", [1, 512], F32)
        rowv = sb("rowv3", [1, 512], F32)
        rowr = sb("rowr3", [1, 512], F32)
        rowmr = sb("rowmr3", [1, 512], F32)
        ident = sb("ident3", [128, 128])
        onesb = sb("onesb3", [128, 128])
        onesf = sb("onesf3", [1, 128], F32)
        ind = sb("ind3", [128, 1], F32)

        p = Plan(nc)
        p.mksem(ctx, "_sf_vector", "_sf_scalar", "ldc", "ldg", "ldd", "ldr",
                "cid", "pcm", "pces", "pcev", "engv", "engs", "wtr", "wte",
                "wrm", "wre", "cnv", "rrm", "cle", "mmf", "res", "sbc",
                "stm", "var", "rsr", "mrw", "rep", "nrm", "st")
        H = p.handles
        env = dict(sT=sT, sbf=sbf, sqb=sqb, rowm=rowm, rowv=rowv, rowr=rowr,
                   rowmr=rowmr, LROW=LROW, LROW2=LROW2, REPv=REPv, REPm=REPm,
                   onesb=onesb, onesf=onesf, gb=gb)

        for dst, src in ((wpe, wpe_d), (bpe, bpe_d), (sttr, sttr_d),
                         (maskr, mask_d), (invpos, inv_d), (invl, invl_d),
                         (e4, e4_d), (r2, r2_d), (gb, gb_d), (ind, ind_d),
                         (ident, ident_d), (onesb, ones_d), (onesf, onesf_d)):
            p.on("sync", lambda e, d=dst, s=src: e.dma_start(
                out=d[:, :], in_=s[:, :]).then_inc(H["ldc"], 16))
            p.inc("ldc", 16)
        p.on("gpsimd", lambda e: e.dma_start(
            out=wfc[:, :].rearrange("p (c n) -> p c n", c=NC),
            in_=wfc_d.rearrange("c p n -> p c n")).then_inc(H["ldg"], 16))
        p.inc("ldg", 16)
        for eng in ("tensor", "scalar", "vector"):
            p.wait(eng, "ldc")
            p.wait(eng, "ldg")
        p.on("vector", lambda e: e.memset(onessc[:, :], 1.0))

        for phase, src_d in (("P", px_d), ("A", x2_d)):
            for c in range(NC):
                p.wait("sync", "cid", p.val("cid"))
                p.on("sync", lambda e, c=c, src=src_d: e.dma_start(
                    out=xc[:, :].rearrange("p (i t) -> p i t", t=128),
                    in_=src.rearrange("i p c t -> p c i t")[:, c]).then_inc(H["ldd"], 16))
                p.inc("ldd", 16)
                p.wait("vector", "ldd")
                p.on("vector", lambda e: e.tensor_tensor_scan(
                    cums[:, :], onessc[:, :], xc[:, :], 0.0, ALU.mult, ALU.add))
                if phase == "P":
                    p.on("vector", lambda e, c=c: e.tensor_copy(
                        ptail[:, c * 30:c * 30 + 30], cums[:, 2018:2048]))
                    p.tag("vector", "cid")
                else:
                    p.on("vector", lambda e, c=c: e.tensor_scalar(
                        cums[:, :], cums[:, :], ptail[:, c * 30 + 29:c * 30 + 30],
                        None, ALU.add))
                    p.on("vector", lambda e, c=c: e.tensor_tensor(
                        ci_ext[:, c * CE + 30:c * CE + 30 + T], cums[:, :],
                        invpos[:, :], ALU.mult))
                    p.on("vector", lambda e, c=c: e.tensor_tensor(
                        ci_ext[:, c * CE:c * CE + 30], ptail[:, c * 30:c * 30 + 30],
                        invl[:, :], ALU.mult))
                    p.tag("vector", "cid")
            if phase == "P":
                p.on("vector", lambda e: e.tensor_scalar(
                    ptail[:, :], ptail[:, :], ind[:, 0:1], None, ALU.mult))
                p.tag("vector", "cid")
        p.wait("tensor", "cid")

        import os as _os
        _stage = _os.environ.get("DBG_L3_STAGE", "")

        def _dbg_finish():
            p.on("vector", lambda e: e.memset(sT[:, :], 0.0))
            p.tag("vector", "nrm")
            p.wait("sync", "nrm")
            p.on("sync", lambda e: e.dma_start(
                out=x3_d[0:2].rearrange("i p c t -> p c i t"),
                in_=sT[:, :].rearrange("p (c i t) -> p c i t", i=2, t=128)).then_inc(H["st"], 16))
            p.inc("st", 16)
            p.wait("sync", "st")
            p.simulate()
            p.emit(ctx)

        if _stage == "A":
            _dbg_finish()
            return nc
        _bl = {"B1": 1, "B2": 2, "B3": 3}.get(_stage, 9)

        for tt in range(NT):
            col = tt * 128
            for hg in range(2):
                p.wait("tensor", "pces", p.val("pces"))
                p.wait("tensor", "pcev", p.val("pcev"))
                for h8 in range(8):
                    h = hg * 8 + h8
                    cch = h // 2
                    rb = (h % 2) * 64
                    p.on("tensor", lambda e, h8=h8, cch=cch, rb=rb, col=col: e.matmul(
                        PC[:, h8 * 62:h8 * 62 + 62],
                        ci_ext[rb:rb + 64, cch * CE + 30 + col: cch * CE + 30 + col + 128],
                        wpe[rb:rb + 64, :], start=True, stop=False))
                    p.on("tensor", lambda e, h8=h8: e.matmul(
                        PC[:, h8 * 62:h8 * 62 + 62], onesb[0:1, 0:128], bpe[:, :],
                        start=False, stop=True))
                p.tag("tensor", "pcm")
                if _bl < 2:
                    p.wait("tensor", "pcm")
                    p.inc("pces"); p.sems["pces"] -= 1
                    continue
                p.wait("scalar", "pcm")
                pcv = PC[:, :].rearrange("p (h n) -> p h n", n=62)
                p.on("scalar", lambda e, hg=hg, pcv=pcv: e.activation(
                    sg[:, hg * 248:hg * 248 + 248], pcv[:, :, 31:62], AF.Sigmoid))
                p.tag("scalar", "pces")
                p.on("scalar", lambda e, hg=hg, pcv=pcv: e.activation(
                    dyn[:, hg * 248:hg * 248 + 248], pcv[:, :, 0:31], AF.Copy))
                p.tag("scalar", "pcev")
            if _bl < 3:
                continue
            p.wait("vector", "pces")
            p.wait("vector", "pcev")
            p.on("vector", lambda e: e.tensor_tensor(
                dyn[:, :], dyn[:, :], sg[:, :], ALU.mult))
            p.on("vector", lambda e: e.tensor_tensor(
                dyn[:, :], dyn[:, :], sttr[:, :], ALU.add))
            p.tag("vector", "engv")
            p.wait("scalar", "engv")
            p.on("scalar", lambda e: e.activation(wexp[:, :], dyn[:, :], AF.Exp))
            p.tag("scalar", "engs")
            p.wait("vector", "engs")
            p.on("vector", lambda e: e.tensor_reduce(
                wsum[:, :], wexp[:, :].rearrange("p (h k) -> p h k", k=KT),
                AX.X, ALU.add))
            p.on("vector", lambda e: e.reciprocal(rcp[:, :], wsum[:, :]))
            p.on("vector", lambda e: e.tensor_copy(rcpb[:, :], rcp[:, :]))
            if tt == 0:
                p.on("vector", lambda e: e.tensor_tensor(
                    wexp[:, :], wexp[:, :], maskr[:, :], ALU.mult))
            p.tag("vector", "engv")
            if _bl < 4:
                continue
            p.wait("tensor", "engv")
            p.wait("tensor", "engs")
            p.wait("tensor", "wte", p.val("wte"))
            for g in range(4):
                p.on("tensor", lambda e, g=g: e.transpose(
                    TRQ[0:124, g * 128:g * 128 + 128], wexp[:, g * 124:g * 124 + 124],
                    ident[:, :]))
            p.on("tensor", lambda e: e.transpose(
                TRQ[0:16, 512:512 + 128], rcpb[:, :], ident[:, :]))
            p.tag("tensor", "wtr")
            p.wait("vector", "wtr")
            for g in range(4):
                p.on("vector", lambda e, g=g, col=col: e.tensor_copy(
                    wq4[g][:, col:col + 128], TRQ[0:124, g * 128:g * 128 + 128]))
            p.on("vector", lambda e, col=col: e.tensor_copy(
                rcpT[:, col:col + 128], TRQ[0:16, 512:512 + 128]))
            p.tag("vector", "wte")
        p.wait("tensor", "wte")

        if _stage.startswith("B"):
            _dbg_finish()
            return nc

        for c in range(NC):
            g = c // 2
            pr = c % 2
            for k in range(KT):
                sel = pr * 31 + k
                if k == 0:
                    p.wait("tensor", "cle", p.val("cle"))
                for blk in range(4):
                    ps = WREP if blk % 2 == 0 else WREP2
                    if blk == 0:
                        p.wait("scalar", "cnv", p.val("cnv"))
                    p.wait("tensor", "wre", p.val("wre") - 1)
                    p.on("tensor", lambda e, sel=sel, g=g, blk=blk, ps=ps: e.matmul(
                        ps[:, :], e4[:, sel * 128:sel * 128 + 128],
                        wq4[g][:, blk * 512:blk * 512 + 512],
                        start=True, stop=True))
                    p.tag("tensor", "wrm")
                    p.wait("scalar", "wrm")
                    p.on("scalar", lambda e, blk=blk, ps=ps: e.activation(
                        wrepb[:, blk * 512:blk * 512 + 512], ps[:, :], AF.Copy))
                    p.tag("scalar", "wre")
                p.wait("vector", "wre")
                p.on("vector", lambda e, c=c, k=k: e.tensor_tensor(
                    prodb[:, :], wrepb[:, :],
                    ci_ext[:, c * CE + k:c * CE + k + T], ALU.mult))
                if k == 0:
                    p.on("vector", lambda e: e.tensor_copy(cacc[:, :], prodb[:, :]))
                else:
                    p.on("vector", lambda e: e.tensor_tensor(
                        cacc[:, :], cacc[:, :], prodb[:, :], ALU.add))
                p.tag("vector", "cnv")
            p.wait("tensor", "wre", p.val("wre"))
            for blk in range(4):
                ps = WREP if blk % 2 == 0 else WREP2
                p.wait("tensor", "cnv", p.val("cnv"))
                p.wait("tensor", "cle", p.val("cle") - 1)
                p.on("tensor", lambda e, c=c, blk=blk, ps=ps: e.matmul(
                    ps[:, :], r2[:, c * 128:c * 128 + 128],
                    rcpT[:, blk * 512:blk * 512 + 512], start=True, stop=True))
                p.tag("tensor", "rrm")
                p.wait("vector", "rrm")
                p.on("vector", lambda e, c=c, blk=blk, ps=ps: e.tensor_tensor(
                    clT[:, c * T + blk * 512:c * T + blk * 512 + 512],
                    cacc[:, blk * 512:blk * 512 + 512], ps[:, :], ALU.mult))
                p.tag("vector", "cle")
        p.wait("tensor", "cle")

        if _stage == "C":
            _dbg_finish()
            return nc

        for j in range(8):
            res_base = p.val("res")
            p.wait("sync", "res", res_base)
            p.on("sync", lambda e, j=j: e.dma_start(
                out=xres[:, :].rearrange("p (c i t) -> p c i t", i=2, t=128),
                in_=x2_d[2 * j:2 * j + 2].rearrange("i p c t -> p c i t")).then_inc(H["ldr"], 16))
            p.inc("ldr", 16)
            p.wait("tensor", "nrm", p.val("nrm"))
            for co in range(NC):
                ps = WFCa if co % 2 == 0 else WREP
                p.wait("tensor", "res", res_base + co - 1)
                for ci in range(NC):
                    p.on("tensor", lambda e, co=co, ci=ci, ps=ps, j=j: e.matmul(
                        ps[:, 0:256],
                        wfc[:, ci * 1024 + co * 128: ci * 1024 + co * 128 + 128],
                        clT[:, ci * T + j * 256: ci * T + j * 256 + 256],
                        start=(ci == 0), stop=(ci == 7)))
                p.tag("tensor", "mmf")
                p.wait("vector", "mmf")
                if co == 0:
                    p.wait("vector", "ldr")
                    p.wait("vector", "st", p.val("st"))
                p.on("vector", lambda e, co=co, ps=ps: e.tensor_tensor(
                    sT[:, co * 256:co * 256 + 256], ps[:, 0:256],
                    xres[:, co * 256:co * 256 + 256], ALU.add))
                p.tag("vector", "res")
            import os as _os
            _dbg = _os.environ.get("DBG_L3_STAGE", "")
            if _dbg == "v":
                p.wait("sync", "res")
                p.on("sync", lambda e, j=j: e.dma_start(
                    out=x3_d[2 * j:2 * j + 2].rearrange("i p c t -> p c i t"),
                    in_=sT[:, :].rearrange("p (c i t) -> p c i t", i=2, t=128)).then_inc(H["st"], 16))
                p.inc("st", 16)
                continue
            emit_ln3(p, H, env, EPS_IN, True, W=256)
            if _dbg == "vn":
                p.wait("sync", "nrm")
                p.on("sync", lambda e, j=j: e.dma_start(
                    out=x3_d[2 * j:2 * j + 2].rearrange("i p c t -> p c i t"),
                    in_=sT[:, :].rearrange("p (c i t) -> p c i t", i=2, t=128)).then_inc(H["st"], 16))
                p.inc("st", 16)
                continue
            p.wait("vector", "nrm")
            for co in range(NC):
                p.on("vector", lambda e, co=co: e.tensor_tensor(
                    sT[:, co * 256:co * 256 + 256], sT[:, co * 256:co * 256 + 256],
                    xres[:, co * 256:co * 256 + 256], ALU.add))
            p.tag("vector", "res")
            emit_ln3(p, H, env, EPS_OUT, False, W=256)
            p.wait("sync", "nrm")
            p.on("sync", lambda e, j=j: e.dma_start(
                out=x3_d[2 * j:2 * j + 2].rearrange("i p c t -> p c i t"),
                in_=sT[:, :].rearrange("p (c i t) -> p c i t", i=2, t=128)).then_inc(H["st"], 16))
            p.inc("st", 16)

        p.wait("sync", "st")
        p.simulate()
        p.emit(ctx)
    return nc


def prep_l3_shared(inputs):
    wpe = np.asarray(inputs["L1_attn_W_pe"], dtype=np.float32)
    bpe = np.asarray(inputs["L1_attn_b_pe"], dtype=np.float32)
    stt = np.asarray(inputs["L1_attn_stt_energy"], dtype=np.float32)
    wfc = np.asarray(inputs["L1_attn_W_fc"], dtype=np.float32)
    gam = np.asarray(inputs["L1_attn_gamma"], dtype=np.float32)
    bet = np.asarray(inputs["L1_attn_beta"], dtype=np.float32)
    sttrep = np.ascontiguousarray(
        np.broadcast_to(stt[0, :, 0, :].reshape(1, 16 * 31), (128, 496)))
    e4 = np.zeros((124, 62 * 128), dtype=BF)
    for prh in range(2):
        for k in range(KT):
            sel = prh * 31 + k
            for m in range(128):
                hq = 2 * prh + m // 64
                e4[hq * 31 + k, sel * 128 + m] = 1
    r2 = np.zeros((16, NC * 128), dtype=BF)
    for c in range(NC):
        for m in range(128):
            r2[2 * c + m // 64, c * 128 + m] = 1
    wfcT = np.ascontiguousarray(wfc.T).reshape(NC, 128, D)
    gbcols = np.zeros((128, 2 * NC), dtype=np.float32)
    for c in range(NC):
        gbcols[:, c] = gam[c * 128:(c + 1) * 128]
        gbcols[:, NC + c] = bet[c * 128:(c + 1) * 128]
    wpe2 = np.ascontiguousarray(np.concatenate([wpe, wpe], axis=0))
    return dict(wpe=wpe2.astype(BF), bpe=bpe.reshape(1, 62).astype(BF),
                sttrep=sttrep, e4sel=e4, r2sel=r2, wfcT=wfcT, gbcols=gbcols,
                identbf=np.eye(128, dtype=BF), onesbf=np.ones((128, 128), dtype=BF),
                onesf=np.ones((1, 128), dtype=np.float32))


def prep_l3_core(core):
    b, h = core // 2, core % 2
    t0 = h * T
    tg = np.arange(T) + t0
    invpos = np.ascontiguousarray(np.broadcast_to(
        (1.0 / (tg + 1)).astype(BF).reshape(1, T), (128, T)))
    if h == 1:
        ivl = 1.0 / (np.arange(30) + t0 - 29)
    else:
        ivl = np.ones(30)
    invlead = np.ascontiguousarray(np.broadcast_to(
        ivl.astype(np.float32).reshape(1, 30), (128, 30)))
    mask = np.ones((128, 496), dtype=BF)
    if h == 0:
        tloc = np.arange(128).reshape(128, 1, 1)
        kk = np.arange(KT).reshape(1, 1, KT)
        mask = np.ascontiguousarray(np.broadcast_to(
            (tloc + kk - 30 >= 0), (128, 16, KT)).reshape(128, 496).astype(BF))
    indv = np.full((128, 1), float(h), dtype=np.float32)
    return dict(invpos=invpos, invlead=invlead, maskrep=mask, ind=indv)



def build_l4():
    nc = bass.Bass("TRN2", target_bir_lowering=False)
    x3_d = nc.dram_tensor("x3st", [NT, 128, NC, 128], F32, kind="ExternalInput").ap()
    aall_d = nc.dram_tensor("aall", [NC, 128, 128], BF16, kind="ExternalInput").ap()
    ball_d = nc.dram_tensor("ball", [128, D], BF16, kind="ExternalInput").ap()
    wg_d = nc.dram_tensor("wg1c", [128, NC, 8], F32, kind="ExternalInput").ap()
    ident_d = nc.dram_tensor("identbf", [128, 128], BF16, kind="ExternalInput").ap()
    ones_d = nc.dram_tensor("onesbf", [128, 128], BF16, kind="ExternalInput").ap()
    onesf_d = nc.dram_tensor("onesf", [1, 128], F32, kind="ExternalInput").ap()
    x4_d = nc.dram_tensor("x4st", [NT, 128, NC, 128], F32, kind="ExternalOutput").ap()

    with ExitStack() as ctx:
        sb = lambda n, s, d=BF16: ctx.enter_context(nc.sbuf_tensor(n, s, d))
        psa = lambda n, s, d=F32: ctx.enter_context(nc.psum_tensor(n, s, d))

        OUT4 = psa("OUT4", [128, 1024])
        HH = psa("HH", [128, 512])
        TR4 = psa("TR4", [128, 512], BF16)
        GA4 = psa("GA4", [128, 8])
        LROW = psa("LROW4", [1, 512])
        LROW2 = psa("LROW24", [1, 512])
        REPS = psa("REPS4", [128, 512])
        REPv = REPS[:, 0:256]
        REPm = REPS[:, 256:512]

        aall = sb("aall_sb", [128, NC * 128])
        ball = sb("ball_sb", [128, D])
        wg = sb("wg1_sb", [128, NC * 8], F32)
        x3b = [sb(f"x3b{j}", [128, NC * 128]) for j in range(2)]
        hw_sb = sb("hw_sb", [128, 128])
        hwT = sb("hwT_sb", [128, 128])
        lg = sb("lg4", [128, 8], F32)
        top8 = sb("top84", [128, 8], F32)
        gsc = sb("gsc4", [128, 8], F32)
        eqa = sb("eqa4", [128, 8], F32)
        eqb = sb("eqb4", [128, 8], F32)
        cw = sb("cw4", [128, 8], F32)
        xres = sb("xres4", [128, NC * 128], F32)
        sT = sb("sT4", [128, NC * 128], F32)
        sbf = sb("sbf4", [128, NC * 128])
        sqb = sb("sqb4", [128, NC * 128])
        rowm = sb("rowm4", [1, 128], F32)
        rowv = sb("rowv4", [1, 128], F32)
        rowr = sb("rowr4", [1, 128], F32)
        rowmr = sb("rowmr4", [1, 128], F32)
        ident = sb("ident4", [128, 128])
        onesb = sb("onesb4", [128, 128])
        onesf = sb("onesf4", [1, 128], F32)

        p = Plan(nc)
        p.mksem(ctx, "_sf_vector", "_sf_scalar", "ldc", "xl0", "xl1", "ldr",
                "gmm", "gcw", "hmm", "hsc", "htr", "hte", "bmm", "res", "sbc",
                "stm", "var", "rsr", "mrw", "rep", "nrm", "st")
        H = p.handles
        env = dict(sT=sT, sbf=sbf, sqb=sqb, rowm=rowm, rowv=rowv, rowr=rowr,
                   rowmr=rowmr, LROW=LROW, LROW2=LROW2, REPv=REPv, REPm=REPm,
                   onesb=onesb, onesf=onesf, gb=None)

        for dst, src, rearr in ((aall, aall_d, "c p n -> p c n"),
                                (ball, ball_d, None), (wg, wg_d, "p c n -> p c n"),
                                (ident, ident_d, None), (onesb, ones_d, None),
                                (onesf, onesf_d, None)):
            if rearr:
                p.on("sync", lambda e, d=dst, s=src, r=rearr: e.dma_start(
                    out=d[:, :].rearrange("p (c n) -> p c n", c=NC),
                    in_=s.rearrange(r)).then_inc(H["ldc"], 16))
            else:
                p.on("sync", lambda e, d=dst, s=src: e.dma_start(
                    out=d[:, :], in_=s[:, :]).then_inc(H["ldc"], 16))
            p.inc("ldc", 16)
        for eng in ("tensor", "scalar", "vector"):
            p.wait(eng, "ldc")

        for tt in range(NT):
            xb = x3b[tt % 2]
            xsem = "xl0" if tt % 2 == 0 else "xl1"
            p.wait("gpsimd", "hmm", p.val("hmm") - 1)
            p.on("gpsimd", lambda e, tt=tt, b=xb, xs=xsem: e.dma_start(
                out=b[:, :].rearrange("p (c t) -> p c t", c=NC),
                in_=x3_d[tt].rearrange("p c t -> p c t")).then_inc(H[xs], 16))
            p.inc(xsem, 16)
            p.wait("sync", "res", p.val("res"))
            p.on("sync", lambda e, tt=tt: e.dma_start(
                out=xres[:, :].rearrange("p (c t) -> p c t", c=NC),
                in_=x3_d[tt].rearrange("p c t -> p c t")).then_inc(H["ldr"], 16))
            p.inc("ldr", 16)
            p.wait("tensor", "ldr")
            p.wait("tensor", "gcw", p.val("gcw"))
            for c in range(NC):
                p.on("tensor", lambda e, c=c: e.matmul(
                    GA4[:, :], xres[:, c * 128:c * 128 + 128],
                    wg[:, c * 8:c * 8 + 8], start=(c == 0), stop=(c == 7)))
            p.tag("tensor", "gmm")
            p.wait("vector", "gmm")
            p.on("vector", lambda e: e.tensor_copy(lg[:, :], GA4[:, :]))
            p.on("vector", lambda e: e.max(out=top8[:, :], in_=lg[:, :]))
            p.on("vector", lambda e: e.tensor_tensor(
                gsc[:, 0:1], top8[:, 1:2], top8[:, 0:1], ALU.subtract))
            p.tag("vector", "gcw")
            p.wait("scalar", "gcw")
            p.on("scalar", lambda e: e.activation(gsc[:, 1:2], gsc[:, 0:1], AF.Exp))
            p.tag("scalar", "gcw")
            p.wait("vector", "gcw")
            p.on("vector", lambda e: e.tensor_scalar(
                gsc[:, 2:3], gsc[:, 1:2], 1.0, None, ALU.add))
            p.on("vector", lambda e: e.reciprocal(gsc[:, 3:4], gsc[:, 2:3]))
            p.on("vector", lambda e: e.tensor_tensor(
                gsc[:, 4:5], gsc[:, 1:2], gsc[:, 3:4], ALU.mult))
            p.on("vector", lambda e: e.tensor_scalar(
                eqa[:, :], lg[:, :], top8[:, 0:1], None, ALU.is_equal))
            p.on("vector", lambda e: e.tensor_scalar(
                eqb[:, :], lg[:, :], top8[:, 1:2], None, ALU.is_equal))
            p.on("vector", lambda e: e.tensor_scalar(
                eqa[:, :], eqa[:, :], gsc[:, 3:4], None, ALU.mult))
            p.on("vector", lambda e: e.tensor_scalar(
                eqb[:, :], eqb[:, :], gsc[:, 4:5], None, ALU.mult))
            p.on("vector", lambda e: e.tensor_tensor(
                cw[:, :], eqa[:, :], eqb[:, :], ALU.add))
            p.tag("vector", "gcw")
            p.wait("tensor", xsem)
            p.wait("tensor", "hsc", p.val("hsc"))
            for c in range(NC):
                p.on("tensor", lambda e, c=c, xb=xb: e.matmul(
                    HH[:, 0:128], xb[:, c * 128:c * 128 + 128],
                    aall[:, c * 128:c * 128 + 128], start=(c == 0), stop=(c == 7)))
            p.tag("tensor", "hmm")
            p.wait("vector", "hmm")
            p.wait("vector", "gcw")
            p.wait("vector", "htr", p.val("htr"))
            for ee in range(8):
                p.on("vector", lambda e, ee=ee: e.tensor_scalar(
                    hw_sb[:, ee * 16:ee * 16 + 16], HH[:, ee * 16:ee * 16 + 16],
                    cw[:, ee:ee + 1], None, ALU.mult))
            p.tag("vector", "hsc")
            p.wait("tensor", "hsc")
            p.wait("tensor", "hte", p.val("hte"))
            p.on("tensor", lambda e: e.transpose(
                TR4[:, 0:128], hw_sb[:, :], ident[:, :]))
            p.tag("tensor", "htr")
            p.wait("vector", "htr")
            p.on("vector", lambda e: e.tensor_copy(hwT[:, :], TR4[:, 0:128]))
            p.tag("vector", "hte")
            p.wait("tensor", "hte")
            p.wait("tensor", "res", p.val("res"))
            for dc in range(NC):
                p.on("tensor", lambda e, dc=dc: e.matmul(
                    OUT4[:, dc * 128:dc * 128 + 128], ball[:, dc * 128:dc * 128 + 128],
                    hwT[:, :], start=True, stop=True))
            p.tag("tensor", "bmm")
            p.wait("vector", "bmm")
            p.wait("vector", "ldr")
            p.wait("vector", "st", p.val("st"))
            for co in range(NC):
                p.on("vector", lambda e, co=co: e.tensor_tensor(
                    sT[:, co * 128:co * 128 + 128], OUT4[:, co * 128:co * 128 + 128],
                    xres[:, co * 128:co * 128 + 128], ALU.add))
            p.tag("vector", "res")
            emit_ln3(p, H, env, EPS_OUT, False, W=128)
            p.wait("sync", "nrm")
            p.on("sync", lambda e, tt=tt: e.dma_start(
                out=x4_d[tt].rearrange("p c t -> p c t"),
                in_=sT[:, :].rearrange("p (c t) -> p c t", c=NC)).then_inc(H["st"], 16))
            p.inc("st", 16)

        p.wait("sync", "st")
        p.simulate()
        p.emit(ctx)
    return nc


def prep_l4_shared(inputs):
    A = np.asarray(inputs["L1_ffn_A"], dtype=np.float32)
    Bm = np.asarray(inputs["L1_ffn_B"], dtype=np.float32)
    wg = np.asarray(inputs["L1_ffn_Wg"], dtype=np.float32)
    aall = np.ascontiguousarray(A.transpose(1, 0, 2).reshape(D, 128)
                                ).reshape(NC, 128, 128).astype(BF)
    ball = np.ascontiguousarray(Bm.reshape(128, D)).astype(BF)
    wg1c = np.ascontiguousarray(wg.reshape(NC, 128, 8).transpose(1, 0, 2))
    return dict(aall=aall, ball=ball, wg1c=wg1c,
                identbf=np.eye(128, dtype=BF),
                onesbf=np.ones((128, 128), dtype=BF),
                onesf=np.ones((1, 128), dtype=np.float32))
